# version 05
# baseline (speedup 1.0000x reference)
import numpy as np
from contextlib import ExitStack

import concourse.bass as bass
import concourse.mybir as mybir
from concourse.bass_utils import run_bass_kernel_spmd

F32 = mybir.dt.float32
I32 = mybir.dt.int32
BF16 = mybir.dt.bfloat16
AF = mybir.ActivationFunctionType
OP = mybir.AluOpType
AX = mybir.AxisListType

S, E, H, DH, M = 4096, 1024, 16, 64, 256
CO, CS = 512, 128
NCH, NSUB = S // CO, CO // CS
LN_EPS = 1e-5
ATT_EPS = 1e-6


class Sem:
    def __init__(self, h):
        self.h = h
        self.n = 0


class Buf:
    def __init__(self, name=""):
        self.name = name
        self.w = None
        self.readers = []


class Prog:
    def __init__(self):
        self.ops = []
        self.sems = {}
        self.sp_chain = True
        self._last_sp = None
        self.small_toks = set()

    def add_sem(self, name, handle):
        self.sems[name] = Sem(handle)

    def op(self, eng, fn, reads=(), writes=(), extra=(), dma=False, dma_n=1,
           sem=None, small=False):
        sem = self.sems[sem or eng]
        waits = []
        for b in reads:
            if b.w is not None:
                waits.append(b.w)
        for b in writes:
            if b.w is not None:
                waits.append(b.w)
            waits.extend(b.readers)
        waits.extend([t for t in extra if t is not None])
        amt = (16 * dma_n) if dma else 1
        if eng == "sp" and self.sp_chain and self._last_sp is not None:
            waits.append(self._last_sp)
        sem.n += amt
        tok = (sem, sem.n)
        if small:
            self.small_toks.add((id(sem), sem.n))
        if eng == "sp":
            self._last_sp = tok
        self.ops.append((eng, fn, tuple(waits), (sem.h, amt)))
        for b in reads:
            b.readers.append(tok)
        for b in writes:
            b.w = tok
            b.readers = []
        return tok

    def emit(self, name, eng, self_waits=False):
        last_wait = {}
        my_sem = self.sems[name]
        skip_self = (name != "sp") and not self_waits
        guard = name in ("dve", "act", "gp")
        own_val = 0
        for i, (e, fn, waits, (semh, amt)) in enumerate(self.ops):
            if e != name:
                continue
            near_v = -1
            for (sem, v) in waits:
                if sem is my_sem:
                    if guard and v >= own_val - 1 and (id(sem), v) in self.small_toks:
                        near_v = max(near_v, v)
                    if skip_self:
                        continue
                if last_wait.get(id(sem), -1) >= v:
                    continue
                last_wait[id(sem)] = v
                eng.wait_ge(sem.h, v)
            if near_v >= 0:
                eng.wait_ge(my_sem.h, near_v)
            inst = fn(eng)
            if inst is not None:
                inst.then_inc(semh, amt)
            own_val += amt


def build_program(n_chunks=NCH, sim_races=False, max_ops=None, dbg=None):
    nc = bass.Bass("TRN2", target_bir_lowering=False, debug=False,
                   detect_race_conditions=sim_races)

    d_x = nc.dram_tensor("x_in", [S, E], F32, kind="ExternalInput").ap()
    d_qw = nc.dram_tensor("q_w", [E, E], F32, kind="ExternalInput").ap()
    d_qb = nc.dram_tensor("q_b", [E], F32, kind="ExternalInput").ap()
    d_kw = nc.dram_tensor("k_w", [E, E], F32, kind="ExternalInput").ap()
    d_kb = nc.dram_tensor("k_b", [E], F32, kind="ExternalInput").ap()
    d_vw = nc.dram_tensor("v_w", [E, E], F32, kind="ExternalInput").ap()
    d_vb = nc.dram_tensor("v_b", [E], F32, kind="ExternalInput").ap()
    d_ow = nc.dram_tensor("out_w", [E, E], F32, kind="ExternalInput").ap()
    d_ob = nc.dram_tensor("out_b", [E], F32, kind="ExternalInput").ap()
    d_proj = nc.dram_tensor("proj", [M, DH], F32, kind="ExternalInput").ap()
    d_gam = nc.dram_tensor("gamma", [E], F32, kind="ExternalInput").ap()
    d_bet = nc.dram_tensor("beta", [E], F32, kind="ExternalInput").ap()
    d_win = nc.dram_tensor("Win", [E, 64], F32, kind="ExternalInput").ap()
    d_bin = nc.dram_tensor("bin_", [64], F32, kind="ExternalInput").ap()
    d_w1 = nc.dram_tensor("W1", [64, 16], F32, kind="ExternalInput").ap()
    d_b1 = nc.dram_tensor("b1", [64, 16], F32, kind="ExternalInput").ap()
    d_w2 = nc.dram_tensor("W2", [64, 16], F32, kind="ExternalInput").ap()
    d_b2 = nc.dram_tensor("b2", [64], F32, kind="ExternalInput").ap()
    d_wout = nc.dram_tensor("Wout", [64, E], F32, kind="ExternalInput").ap()
    d_bout = nc.dram_tensor("bout", [E], F32, kind="ExternalInput").ap()
    d_y = nc.dram_tensor("x_out", [S, E], F32, kind="ExternalOutput").ap()

    P = Prog()

    with ExitStack() as ctx:
        En = ctx.enter_context

        def sb(name, shape, dt):
            return En(nc.sbuf_tensor(name, shape, dt))

        xres = [sb("xres0", [128, 4, E], F32), sb("xres1", [128, 4, E], F32)]
        wq_sb = sb("wq", [128, 8, E], BF16)
        wk_sb = sb("wk", [128, 8, E], BF16)
        wv_sb = sb("wv", [128, 8, E], BF16)
        wo_sb = sb("wo", [128, 8, E], BF16)
        win_sb = sb("win", [128, 8, 64], BF16)
        w1e_sb = sb("w1e", [128, E], BF16)
        w2e_sb = sb("w2e", [128, 8, 64], BF16)
        wout_sb = sb("wout", [128, E], BF16)
        projz = sb("projz", [128, 2, M], BF16)
        gam_rep = sb("gam_rep", [128, E], F32)
        bet_rep = sb("bet_rep", [128, E], F32)
        rows_bf = sb("rows_bf", [1, 3 * E], BF16)
        ones_row = sb("ones_row", [1, 128], BF16)
        onesf_row = sb("onesf_row", [1, 128], F32)
        qb_fm = sb("qb_fm", [128, 8], F32)
        kb_fm = sb("kb_fm", [128, 8], F32)
        bin_sb = sb("binv", [64, 1], F32)
        b1f_sb = sb("b1f", [128, 8], F32)
        b2_sb = sb("b2v", [64, 1], F32)
        w2f_sb = sb("w2f", [128, 8], F32)
        eps5 = sb("eps5", [128, 1], F32)
        eps6 = sb("eps6", [128, 1], F32)
        ident = sb("ident", [128, 128], BF16)
        mask4 = sb("mask4", [128, 4, 128], BF16)
        ind_sb = sb("ind", [128, 8, 16], BF16)
        xln_st = sb("xlnst", [128, E], BF16)
        xlnfm = sb("xlnfm", [128, 8, CO], BF16)
        qfm_sb = sb("qfm", [128, 8, CO], BF16)
        kfm_sb = sb("kfm", [128, 8, CO], BF16)
        v_sb = sb("v", [128, 16, 66], BF16)
        vs_sb = sb("vs", [128, 16, 66], BF16)
        cx_sb = sb("cx", [128, 4, 16], BF16)
        qp_sb = sb("qp", [128, 16, 2, 128], BF16)
        kpf_sb = sb("kpf", [128, 16, 2, 128], BF16)
        kpt_sb = sb("kpt", [128, 16, M], BF16)
        state_sb = sb("state", [128, 16, 2, 66], BF16)
        ksum_sb = sb("ksum", [128, 32], F32)
        scores_sb = sb("scores", [128, 2, 4, 128], BF16)
        attn_tm = sb("attn_tm", [128, 16 * 64], BF16)
        attnT = sb("attnT", [128, 8, CO], BF16)
        s_sb = sb("s_fm", [128, CO], BF16)
        h_sb = sb("h_fm", [128, 2, CO], BF16)
        g_sb = sb("g_fm", [128, CO], BF16)
        stats_sb = sb("stats", [128, 4, 12], F32)
        agg_sb = sb("agg", [128, 24], F32)
        rstd_sb = sb("rstd", [128, 4], F32)
        recden = sb("recden", [128, 16], F32)

        state_ps = En(nc.psum_tensor("state_ps", [128, 2048], F32))
        pbA = En(nc.psum_tensor("pbA", [128, 512], F32))
        pbB = En(nc.psum_tensor("pbB", [128, 512], F32))
        pbC = En(nc.psum_tensor("pbC", [128, 512], F32))
        pbT = En(nc.psum_tensor("pbT", [128, 1024], BF16))

        for nm in ("sp", "pe", "dve", "act", "gp", "x0", "x1", "s0", "s1"):
            P.add_sem(nm, En(nc.semaphore("sem_" + nm)))

        B = lambda n: Buf(n)
        bx = [B("x0"), B("x1")]
        bw = {k: B(k) for k in ["wq", "wk", "wv", "wo", "win", "w1e", "w2e",
                                "wout", "projTd", "gam", "bet", "rows", "ones",
                                "qb", "kb", "bin", "b1f", "b2", "w2f", "eps5",
                                "eps6", "ident", "mask", "ind"]}
        bstage = [B("st0"), B("st1"), B("st2"), B("st3")]
        bxlnst = B("xlnst")
        bxlnfm = B("xlnfm")
        bqfm, bkfm = B("qfm"), B("kfm")
        bv = B("v")
        bvs = B("vs")
        bcx = B("cx")
        bqp, bkpf, bkpt = B("qp"), B("kpf"), B("kpt")
        bstate_sb, bksum = B("state_sb"), B("ksum")
        bscores = [B("sc0"), B("sc1")]
        battn_tm, battnT = B("attn_tm"), B("attnT")
        bs_sb, bh, bg = B("s"), B("h"), B("g")
        bstats, bagg, brstd = B("stats"), B("agg"), B("rstd")
        brec = B("recden")
        bA, bB_, bC, bT = B("pbA"), B("pbB"), B("pbC"), B("pbT")
        bst_ps = B("state_ps")

        stage = xres[0]
        scr_t = xres[1]
        bscr = bx[1]

        P.op("dve", lambda e: e.memset(ones_row[:], 1.0), writes=[bw["ones"]])
        P.op("dve", lambda e: e.memset(onesf_row[:], 1.0), writes=[bw["ones"]])
        P.op("dve", lambda e: e.memset(eps5[:], LN_EPS), writes=[bw["eps5"]])
        P.op("dve", lambda e: e.memset(eps6[:], ATT_EPS), writes=[bw["eps6"]])
        P.op("dve", lambda e: e.memset(ksum_sb[:], 0.0), writes=[bksum])
        P.op("dve", lambda e: e.memset(state_sb[:], 0.0), writes=[bstate_sb])
        P.op("dve", lambda e: e.memset(projz[:], 0.0), writes=[bw["projTd"]])
        P.op("dve", lambda e: e.memset(s_sb.ap()[64:128, :], 0.0), writes=[bs_sb])
        P.op("dve", lambda e: e.memset(g_sb.ap()[64:128, :], 0.0), writes=[bg])
        P.op("dve", lambda e: e.memset(w1e_sb.ap()[64:128, :], 0.0), writes=[bw["w1e"]])
        P.op("dve", lambda e: e.memset(wout_sb.ap()[64:128, :], 0.0), writes=[bw["wout"]])

        P.op("gp", lambda e: e.iota(scr_t[:, 3, 0:128], [[1, 128]], base=0,
                                    channel_multiplier=-1,
                                    allow_small_or_imprecise_dtypes=True),
             writes=[bscr])
        P.op("dve", lambda e: e.tensor_scalar(mask4[:, 0, :], scr_t[:, 3, 0:128],
                                              0.0, None, OP.is_ge),
             reads=[bscr], writes=[bw["mask"]])
        P.op("dve", lambda e: e.tensor_scalar(ident[:], scr_t[:, 3, 0:128],
                                              0.0, None, OP.is_equal),
             reads=[bscr], writes=[bw["ident"]])
        for r in range(1, 4):
            P.op("dve", lambda e, r=r: e.tensor_copy(mask4[:, r, :], mask4[:, 0, :]),
                 reads=[bw["mask"]], writes=[bw["mask"]])
        P.op("dve", lambda e: e.memset(ind_sb[:], 0.0), writes=[bw["ind"]])
        for ei in range(8):
            P.op("dve", lambda e, ei=ei: e.memset(
                ind_sb.ap()[0:64, ei, 2 * ei:2 * ei + 1], 1.0), writes=[bw["ind"]])
            P.op("dve", lambda e, ei=ei: e.memset(
                ind_sb.ap()[64:128, ei, 2 * ei + 1:2 * ei + 2], 1.0), writes=[bw["ind"]])

        def load_big(dsrc, dst_sb, dstbuf):
            for ei in range(8):
                slot = ei % 4
                src = bass.AP(dsrc.tensor, ei * 128 * E, [[E, 128], [1, E]])
                P.op("sp", lambda e, src=src, slot=slot: e.dma_start(
                    out=stage[:, slot, :], in_=src), writes=[bstage[slot]],
                    dma=True)
                if ei % 2 == 0:
                    P.op("dve", lambda e, slot=slot, ei=ei: e.tensor_copy(
                        dst_sb[:, ei, :], stage[:, slot, :]),
                        reads=[bstage[slot]], writes=[dstbuf])
                else:
                    P.op("act", lambda e, slot=slot, ei=ei: e.copy(
                        dst_sb[:, ei, :], stage[:, slot, :]),
                        reads=[bstage[slot]], writes=[dstbuf])

        load_big(d_qw, wq_sb, bw["wq"])
        load_big(d_kw, wk_sb, bw["wk"])
        load_big(d_vw, wv_sb, bw["wv"])
        load_big(d_ow, wo_sb, bw["wo"])

        for ei in range(8):
            src = bass.AP(d_win.tensor, ei * 128 * 64, [[64, 128], [1, 64]])
            P.op("sp", lambda e, src=src: e.dma_start(out=stage[:, 0, 0:64], in_=src),
                 writes=[bstage[0]], dma=True)
            P.op("dve", lambda e, ei=ei: e.tensor_copy(win_sb[:, ei, :],
                                                       stage[:, 0, 0:64]),
                 reads=[bstage[0]], writes=[bw["win"]])
        src = bass.AP(d_wout.tensor, 0, [[E, 64], [1, E]])
        P.op("sp", lambda e, src=src: e.dma_start(out=stage.ap()[0:64, 0, :], in_=src),
             writes=[bstage[0]], dma=True)
        P.op("dve", lambda e: e.tensor_copy(wout_sb.ap()[0:64, :],
                                            stage.ap()[0:64, 0, :]),
             reads=[bstage[0]], writes=[bw["wout"]])

        def ld_small(dsrc, dst_ap, bufw, apdesc):
            src = bass.AP(dsrc.tensor, 0, apdesc)
            P.op("sp", lambda e, src=src, dst_ap=dst_ap: e.dma_start(
                out=dst_ap, in_=src, allow_slow_non_contiguous=True),
                writes=[bufw], dma=True)

        ld_small(d_qb, qb_fm[:], bw["qb"], [[1, 128], [128, 8]])
        ld_small(d_kb, kb_fm[:], bw["kb"], [[1, 128], [128, 8]])
        ld_small(d_bin, bin_sb[:], bw["bin"], [[1, 64], [1, 1]])
        ld_small(d_b1, b1f_sb[:], bw["b1f"], [[1, 128], [128, 8]])
        ld_small(d_b2, b2_sb[:], bw["b2"], [[1, 64], [1, 1]])
        ld_small(d_w2, w2f_sb[:], bw["w2f"], [[1, 128], [128, 8]])

        def ld_row_to(dsrc, dst_row_ap, n=E):
            src = bass.AP(dsrc.tensor, 0, [[n, 1], [1, n]])
            P.op("sp", lambda e, src=src, n=n: e.dma_start(
                out=scr_t.ap()[0:1, 3, 0:n], in_=src), writes=[bscr], dma=True)
            P.op("dve", lambda e, dst_row_ap=dst_row_ap, n=n: e.tensor_copy(
                dst_row_ap, scr_t.ap()[0:1, 3, 0:n]),
                reads=[bscr], writes=[bw["rows"]])

        def bcast_row_f32(dsrc, dst_sb, dstbuf):
            srcap = bass.AP(dsrc.tensor, 0, [[E, 1], [1, E]])
            P.op("sp", lambda e, srcap=srcap: e.dma_start(
                out=scr_t.ap()[0:1, 3, 0:E], in_=srcap), writes=[bscr], dma=True)
            for hf in range(2):
                pb = pbA if hf == 0 else pbB
                pbb = bA if hf == 0 else bB_
                P.op("pe", lambda e, hf=hf, pb=pb: e.matmul(
                    pb[:, 0:512], onesf_row[:],
                    scr_t.ap()[0:1, 3, hf * 512:(hf + 1) * 512],
                    start=True, stop=True),
                    reads=[bscr, bw["ones"]], writes=[pbb])
                P.op("dve", lambda e, hf=hf, pb=pb, dst_sb=dst_sb: e.tensor_copy(
                    dst_sb[:, hf * 512:(hf + 1) * 512], pb[:, 0:512]),
                    reads=[bA if hf == 0 else bB_], writes=[dstbuf])

        bcast_row_f32(d_gam, gam_rep, bw["gam"])
        bcast_row_f32(d_bet, bet_rep, bw["bet"])

        ld_row_to(d_w1, rows_bf.ap()[0:1, 0:E], n=1024)
        for hf in range(2):
            pb = pbA if hf == 0 else pbB
            pbb = bA if hf == 0 else bB_
            P.op("pe", lambda e, hf=hf, pb=pb: e.matmul(
                pb.ap()[0:64, 0:512], ones_row.ap()[0:1, 0:64],
                rows_bf.ap()[0:1, hf * 512:(hf + 1) * 512],
                start=True, stop=True),
                reads=[bw["rows"], bw["ones"]], writes=[pbb])
        P.op("gp", lambda e: e.iota(scr_t.ap()[0:64, 0, :], [[1, 1024]], base=0,
                                    channel_multiplier=-16,
                                    allow_small_or_imprecise_dtypes=True),
             writes=[bscr])
        P.op("dve", lambda e: e.tensor_scalar(scr_t.ap()[0:64, 1, :],
                                              scr_t.ap()[0:64, 0, :],
                                              0.0, None, OP.is_ge),
             writes=[bscr])
        P.op("dve", lambda e: e.tensor_scalar(scr_t.ap()[0:64, 2, :],
                                              scr_t.ap()[0:64, 0, :],
                                              15.0, None, OP.is_le),
             writes=[bscr])
        P.op("dve", lambda e: e.tensor_tensor(scr_t.ap()[0:64, 1, :],
                                              scr_t.ap()[0:64, 1, :],
                                              scr_t.ap()[0:64, 2, :], OP.mult),
             writes=[bscr])
        for hf in range(2):
            pb = pbA if hf == 0 else pbB
            pbb = bA if hf == 0 else bB_
            P.op("dve", lambda e, hf=hf, pb=pb: e.tensor_tensor(
                w1e_sb.ap()[0:64, hf * 512:(hf + 1) * 512],
                scr_t.ap()[0:64, 1, hf * 512:(hf + 1) * 512],
                pb.ap()[0:64, 0:512], OP.mult),
                reads=[bscr, bA if hf == 0 else bB_], writes=[bw["w1e"]])

        P.op("gp", lambda e: e.iota(scr_t[:, 3, 256:320], [[16, 64]], base=0,
                                    channel_multiplier=-1,
                                    allow_small_or_imprecise_dtypes=True),
             writes=[bscr])
        for jt in range(8):
            P.op("dve", lambda e: e.memset(scr_t[:, 3, 0:64], 1.0), writes=[bscr])
            P.op("dve", lambda e, jt=jt: e.tensor_scalar(
                scr_t[:, 3, 64:128], scr_t[:, 3, 0:64], w2f_sb[:, jt:jt + 1],
                None, OP.mult),
                reads=[bw["w2f"]], writes=[bscr])
            P.op("dve", lambda e, jt=jt: e.tensor_scalar(
                scr_t[:, 3, 128:192], scr_t[:, 3, 256:320],
                float(128 * jt - 15), None, OP.is_ge),
                writes=[bscr])
            P.op("dve", lambda e, jt=jt: e.tensor_scalar(
                scr_t[:, 3, 192:256], scr_t[:, 3, 256:320],
                float(128 * jt), None, OP.is_le),
                writes=[bscr])
            P.op("dve", lambda e, jt=jt: e.tensor_tensor(
                scr_t[:, 3, 128:192], scr_t[:, 3, 128:192], scr_t[:, 3, 192:256],
                OP.mult),
                writes=[bscr])
            P.op("dve", lambda e, jt=jt: e.tensor_tensor(
                w2e_sb[:, jt, :], scr_t[:, 3, 128:192], scr_t[:, 3, 64:128],
                OP.mult),
                reads=[bscr], writes=[bw["w2e"]])

        for mt in range(2):
            src = bass.AP(d_proj.tensor, mt * 128 * 64, [[64, 128], [1, 64]])
            P.op("sp", lambda e, src=src: e.dma_start(out=scr_t[:, 3, 0:64], in_=src),
                 writes=[bscr], dma=True)
            P.op("dve", lambda e: e.tensor_copy(attn_tm.ap()[:, 0:64],
                                                scr_t[:, 3, 0:64]),
                 reads=[bscr], writes=[battn_tm])
            def ftp(e, mt=mt):
                e.transpose(pbT.ap()[0:64, mt * 128:(mt + 1) * 128],
                            attn_tm.ap()[:, 0:64], ident[:])
                return e.drain()
            P.op("pe", ftp, reads=[battn_tm, bw["ident"]], writes=[bT])
        P.op("dve", lambda e: e.tensor_copy(projz.ap()[0:64, 0, :].bitcast(I32),
                                            pbT.ap()[0:64, 0:256].bitcast(I32)),
             reads=[bT], writes=[bw["projTd"]])
        P.op("sp", lambda e: e.dma_start(out=projz.ap()[64:128, 1, :],
                                         in_=projz.ap()[0:64, 0, :]),
             reads=[bw["projTd"]], writes=[bw["projTd"]], dma=True)

        ld_row_to(d_vb, rows_bf.ap()[0:1, 0:E])
        ld_row_to(d_ob, rows_bf.ap()[0:1, E:2 * E])
        ld_row_to(d_bout, rows_bf.ap()[0:1, 2 * E:3 * E])
        VB, OB, BO = 0, E, 2 * E

        pb_rot = [(pbA, bA), (pbB, bB_), (pbC, bC)]
        rot_i = [0]

        def next_pb():
            pb, pbb = pb_rot[rot_i[0] % 3]
            rot_i[0] += 1
            return pb, pbb

        def ln_full(src_t, src_buf, eps_t, eps_b):
            def f(e):
                r = None
                for j in range(4):
                    r = e.bn_stats(stats_sb[:, j, 0:6], src_t[:, j, 0:512])
                    r = e.bn_stats(stats_sb[:, j, 6:12], src_t[:, j, 512:1024])
                return r
            P.op("dve", f, reads=[src_buf], writes=[bstats])
            means = bass.AP(stats_sb.ap().tensor, 1, [[48, 128], [12, 4], [3, 4]])
            m2s = bass.AP(stats_sb.ap().tensor, 2, [[48, 128], [12, 4], [3, 4]])
            s1 = agg_sb[:, 0:4]
            s2 = agg_sb[:, 4:8]
            s3 = agg_sb[:, 8:12]
            nmu = agg_sb[:, 12:16]
            var = agg_sb[:, 16:20]

            stt = stats_sb.ap().tensor

            def g1(e):
                r = None
                for j in range(4):
                    e.tensor_reduce(s1[:, j:j + 1],
                                    bass.AP(stt, 12 * j + 1, [[48, 128], [3, 4]]),
                                    AX.X, OP.add)
                    r = e.tensor_reduce(s2[:, j:j + 1],
                                        bass.AP(stt, 12 * j + 2, [[48, 128], [3, 4]]),
                                        AX.X, OP.add)
                return r
            P.op("dve", g1, reads=[bstats], writes=[bagg], small=True)
            P.op("dve", lambda e: e.tensor_tensor(recden[:], means, means, OP.mult),
                 reads=[bstats], writes=[brec], small=True)

            def g2(e):
                r = None
                for j in range(4):
                    r = e.tensor_reduce(s3[:, j:j + 1],
                                        recden[:, 4 * j:4 * j + 4], AX.X, OP.add)
                return r
            P.op("dve", g2, reads=[brec], writes=[bagg], small=True)
            P.op("dve", lambda e: e.tensor_scalar(nmu, s1, -0.25, None, OP.mult),
                 writes=[bagg], small=True)
            P.op("dve", lambda e: e.scalar_tensor_tensor(
                agg_sb[:, 20:24], s3, 256.0, s2, OP.mult, OP.add),
                writes=[bagg], small=True)
            P.op("dve", lambda e: e.tensor_tensor(s3, nmu, nmu, OP.mult),
                 writes=[bagg], small=True)
            P.op("dve", lambda e: e.tensor_scalar(var, agg_sb[:, 20:24],
                                                  1.0 / 1024.0, None, OP.mult),
                 writes=[bagg], small=True)
            P.op("dve", lambda e: e.tensor_tensor(var, var, s3, OP.subtract),
                 writes=[bagg], small=True)
            P.op("act", lambda e: e.activation(rstd_sb[:], var, AF.Sqrt,
                                               bias=eps_t[:], scale=1.0),
                 reads=[bagg, eps_b], writes=[brstd], small=True)
            P.op("dve", lambda e: e.reciprocal(rstd_sb[:], rstd_sb[:]),
                 writes=[brstd], small=True)
            return nmu

        def ln_apply_j(src_t, src_buf, nmu, j, out_ap, out_buf):
            P.op("dve", lambda e, j=j, out_ap=out_ap: e.tensor_scalar(
                out_ap, src_t[:, j, :], nmu[:, j:j + 1], rstd_sb[:, j:j + 1],
                OP.add, OP.mult),
                reads=[src_buf, bagg, brstd], writes=[out_buf])

        def transpose_to_fm(src_ap, src_buf, j):
            def f(e, src_ap=src_ap):
                for ec in range(8):
                    e.transpose(pbT[:, ec * 128:(ec + 1) * 128],
                                src_ap[:, ec * 128:(ec + 1) * 128], ident[:])
                return e.drain()
            P.op("pe", f, reads=[src_buf, bw["ident"]], writes=[bT])
            dst = bass.AP(xlnfm.ap().tensor, j * 128,
                          [[8 * CO, 128], [CO, 8], [1, 128]]).bitcast(I32)
            P.op("dve", lambda e, dst=dst: e.tensor_copy(dst, pbT.ap().bitcast(I32)),
                 reads=[bT], writes=[bxlnfm])

        def ffn_sublayer(xc_t, xc_buf):
            nmu = ln_full(xc_t, xc_buf, eps5, bw["eps5"])
            for j in range(4):
                ln_apply_j(xc_t, xc_buf, nmu, j, xln_st[:], bxlnst)
                transpose_to_fm(xln_st.ap(), bxlnst, j)
            pb, pbb = next_pb()

            def fs(e, pb=pb):
                r = None
                for ei in range(8):
                    r = e.matmul(pb.ap()[0:64, 0:512], win_sb[:, ei, :],
                                 xlnfm[:, ei, :], start=(ei == 0), stop=(ei == 7))
                return r
            P.op("pe", fs, reads=[bxlnfm, bw["win"]], writes=[pbb])
            P.op("act", lambda e, pb=pb: e.activation(
                s_sb.ap()[0:64, :], pb.ap()[0:64, 0:512], AF.Identity,
                bias=bin_sb[:], scale=1.0),
                reads=[pbb, bw["bin"]], writes=[bs_sb])
            pg, pgb = next_pb()
            for jt in range(8):
                ph, phb = next_pb()
                if ph is pg:
                    ph, phb = next_pb()
                P.op("pe", lambda e, ph=ph, jt=jt: e.matmul(
                    ph[:, 0:512], w1e_sb.ap()[:, jt * 128:(jt + 1) * 128],
                    s_sb[:], start=True, stop=True),
                    reads=[bs_sb, bw["w1e"]], writes=[phb])
                hslot = jt % 2
                P.op("act", lambda e, ph=ph, jt=jt, hslot=hslot: e.activation(
                    h_sb[:, hslot, :], ph[:, 0:512], AF.Relu,
                    bias=b1f_sb[:, jt:jt + 1], scale=1.0),
                    reads=[phb, bw["b1f"]], writes=[bh])
                P.op("pe", lambda e, pg=pg, jt=jt, hslot=hslot: e.matmul(
                    pg.ap()[0:64, 0:512], w2e_sb[:, jt, :], h_sb[:, hslot, :],
                    start=(jt == 0), stop=(jt == 7)),
                    reads=[bh, bw["w2e"]], writes=[pgb])
            P.op("act", lambda e, pg=pg: e.activation(
                g_sb.ap()[0:64, :], pg.ap()[0:64, 0:512], AF.Identity,
                bias=b2_sb[:], scale=1.0),
                reads=[pgb, bw["b2"]], writes=[bg])
            for sj in range(4):
                for hf in range(2):
                    pb2, pbb2 = next_pb()

                    def fo(e, pb2=pb2, sj=sj, hf=hf):
                        e.matmul(pb2[:, 0:512],
                                 g_sb.ap()[:, sj * 128:(sj + 1) * 128],
                                 wout_sb.ap()[:, hf * 512:(hf + 1) * 512],
                                 start=True, stop=False)
                        return e.matmul(
                            pb2[:, 0:512], ones_row[:],
                            rows_bf.ap()[0:1, BO + hf * 512:BO + (hf + 1) * 512],
                            start=False, stop=True)
                    P.op("pe", fo, reads=[bg, bw["wout"], bw["ones"], bw["rows"]],
                         writes=[pbb2])
                    P.op("dve", lambda e, pb2=pb2, sj=sj, hf=hf:
                         e.scalar_tensor_tensor(
                             xc_t[:, sj, hf * 512:(hf + 1) * 512], pb2[:, 0:512],
                             0.5, xc_t[:, sj, hf * 512:(hf + 1) * 512],
                             OP.mult, OP.add),
                         reads=[pbb2], writes=[xc_buf])

        def attn_sublayer(c, xc_t, xc_buf, res_t, res_buf):
            nmu = ln_full(xc_t, xc_buf, eps5, bw["eps5"])
            for j in range(4):
                ln_apply_j(xc_t, xc_buf, nmu, j, xln_st[:], bxlnst)
                transpose_to_fm(xln_st.ap(), bxlnst, j)
            for (wsb, wbuf, bias_sb, bias_buf, dst, dbuf) in (
                    (wq_sb, bw["wq"], qb_fm, bw["qb"], qfm_sb, bqfm),
                    (wk_sb, bw["wk"], kb_fm, bw["kb"], kfm_sb, bkfm)):
                for eo in range(8):
                    pb, pbb = next_pb()

                    def fq(e, pb=pb, wsb=wsb, eo=eo):
                        r = None
                        for ei in range(8):
                            r = e.matmul(pb[:, 0:512],
                                         wsb[:, ei, eo * 128:(eo + 1) * 128],
                                         xlnfm[:, ei, :],
                                         start=(ei == 0), stop=(ei == 7))
                        return r
                    P.op("pe", fq, reads=[bxlnfm, wbuf], writes=[pbb])
                    P.op("act", lambda e, pb=pb, eo=eo, dst=dst, bias_sb=bias_sb:
                         e.activation(dst[:, eo, :], pb[:, 0:512], AF.Identity,
                                      bias=bias_sb[:, eo:eo + 1], scale=1.0),
                         reads=[pbb, bias_buf], writes=[dbuf])
            pc, pcb = next_pb()
            for eo in range(8):
                P.op("dve", lambda e, eo=eo: e.tensor_tensor(
                    attn_tm.ap()[:, 0:512], kfm_sb[:, eo, :], kfm_sb[:, eo, :],
                    OP.mult),
                    reads=[bkfm], writes=[battn_tm])

                def fc(e, eo=eo, pc=pc):
                    r = None
                    for sj in range(4):
                        r = e.matmul(pc[:, 64 + sj * 16:64 + sj * 16 + 16],
                                     attn_tm.ap()[:, sj * 128:(sj + 1) * 128],
                                     ind_sb[:, eo, :],
                                     start=(eo == 0 and sj == 0),
                                     stop=(eo == 7 and sj == 3))
                    return r
                P.op("pe", fc, reads=[battn_tm, bw["ind"]], writes=[pcb])
            P.op("act", lambda e, pc=pc: e.activation(
                cx_sb.ap()[:, :, :], pc[:, 64:128], AF.Exp, bias=0.0,
                scale=-1.0 / 128.0),
                reads=[pcb], writes=[bcx], small=True)

            for sj in range(NSUB):
                sub_g = c * NSUB + sj
                for hf in range(2):
                    pb, pbb = next_pb()

                    def fv(e, pb=pb, sj=sj, hf=hf):
                        for ei in range(8):
                            e.matmul(pb[:, 0:512],
                                     xlnfm[:, ei, sj * 128:(sj + 1) * 128],
                                     wv_sb[:, ei, hf * 512:(hf + 1) * 512],
                                     start=(ei == 0), stop=False)
                        return e.matmul(
                            pb[:, 0:512], ones_row[:],
                            rows_bf.ap()[0:1, VB + hf * 512:VB + (hf + 1) * 512],
                            start=False, stop=True)
                    P.op("pe", fv, reads=[bxlnfm, bw["wv"], bw["ones"], bw["rows"]],
                         writes=[pbb])
                    dstv = bass.AP(v_sb.ap().tensor, hf * 8 * 66,
                                   [[16 * 66, 128], [66, 8], [1, 64]])
                    P.op("act", lambda e, pb=pb, dstv=dstv: e.copy(dstv, pb[:, 0:512]),
                         reads=[pbb], writes=[bv])
                for hw in range(8):
                    pb, pbb = next_pb()

                    def fm_(e, pb=pb, hw=hw, sj=sj):
                        r = None
                        for hh in range(2):
                            for mc in range(2):
                                r = e.matmul(
                                    pb[:, hh * 256 + mc * 128:
                                       hh * 256 + (mc + 1) * 128],
                                    projz[:, hh, mc * 128:(mc + 1) * 128],
                                    qfm_sb[:, hw, sj * 128:(sj + 1) * 128],
                                    start=(hh == 0 and mc == 0),
                                    stop=(hh == 1 and mc == 1))
                        return r
                    P.op("pe", fm_, reads=[bqfm, bw["projTd"]], writes=[pbb])
                    P.op("act", lambda e, pb=pb, hw=hw: e.activation(
                        qp_sb[:, 2 * hw:2 * hw + 2, :, :], pb[:], AF.Exp,
                        bias=0.0, scale=0.125),
                        reads=[pbb], writes=[bqp])
                for hw in range(8):
                    pb, pbb = next_pb()

                    def fk(e, pb=pb, hw=hw, sj=sj):
                        r = None
                        for hh in range(2):
                            r = e.matmul(pb[:, hh * 256:(hh + 1) * 256],
                                         kfm_sb[:, hw, sj * 128:(sj + 1) * 128],
                                         projz[:, hh, :],
                                         start=(hh == 0), stop=(hh == 1))
                        return r
                    P.op("pe", fk, reads=[bkfm, bw["projTd"]], writes=[pbb])
                    P.op("act", lambda e, pb=pb, hw=hw: e.activation(
                        kpt_sb[:, 2 * hw:2 * hw + 2, :], pb[:], AF.Exp,
                        bias=0.0, scale=0.125),
                        reads=[pbb], writes=[bkpt])
                    if hw % 2 == 1:
                        g4 = hw // 2

                        def ft(e, g4=g4):
                            for hh in range(4):
                                h = 4 * g4 + hh
                                for mc in range(2):
                                    e.transpose(
                                        pbT[:, (hh * 2 + mc) * 128:
                                            (hh * 2 + mc + 1) * 128],
                                        kpt_sb.ap()[:, h, mc * 128:(mc + 1) * 128],
                                        ident[:])
                            return e.drain()
                        P.op("pe", ft, reads=[bkpt, bw["ident"]], writes=[bT])
                        P.op("dve", lambda e, g4=g4: e.tensor_copy(
                            kpf_sb[:, 4 * g4:4 * g4 + 4, :, :].bitcast(I32),
                            pbT.ap().bitcast(I32)),
                            reads=[bT], writes=[bkpf])
                P.op("dve", lambda e, sj=sj: e.tensor_tensor(
                    bass.AP(vs_sb.ap().tensor, 0,
                            [[16 * 66, 128], [66, 16], [1, 64]]),
                    bass.AP(v_sb.ap().tensor, 0,
                            [[16 * 66, 128], [66, 16], [1, 64]]),
                    bass.AP(cx_sb.ap().tensor, sj * 16,
                            [[4 * 16, 128], [1, 16], [0, 64]]),
                    OP.mult),
                    reads=[bv, bcx], writes=[bvs])
                P.op("dve", lambda e, sj=sj: e.tensor_copy(
                    bass.AP(vs_sb.ap().tensor, 64,
                            [[16 * 66, 128], [66, 16], [1, 1]]),
                    cx_sb[:, sj, :]),
                    reads=[bcx], writes=[bvs], small=True)
                def fks(e, sj=sj):
                    r = None
                    for h in range(16):
                        for mc in range(2):
                            u = 2 * h + mc
                            r = e.matmul(pbC[:, 260 + u:260 + u + 1],
                                         kpt_sb.ap()[:, h, mc * 128:(mc + 1) * 128],
                                         cx_sb[:, sj, h:h + 1],
                                         start=(u == 0), stop=(u == 31))
                    return r
                P.op("pe", fks, reads=[bkpt, bcx], writes=[bC])
                P.op("dve", lambda e: e.tensor_tensor(
                    ksum_sb[:, 0:32], ksum_sb[:, 0:32], pbC[:, 260:292], OP.add),
                    reads=[bC], writes=[bksum], small=True)
                pbS = [(pbA, bA), (pbB, bB_)]
                for w in range(4):
                    ps, psb = pbS[w % 2]
                    wslot = w % 2

                    def fsc(e, ps=ps, w=w):
                        r = None
                        for hw in range(4):
                            h = 4 * w + hw
                            for mc in range(2):
                                r = e.matmul(ps[:, hw * 128:(hw + 1) * 128],
                                             kpf_sb[:, h, mc, :],
                                             qp_sb[:, h, mc, :],
                                             start=(hw == 0 and mc == 0),
                                             stop=(hw == 3 and mc == 1))
                        return r
                    P.op("pe", fsc, reads=[bkpf, bqp], writes=[psb])
                    P.op("dve", lambda e, ps=ps, wslot=wslot: e.tensor_tensor(
                        scores_sb[:, wslot, :, :], ps[:], mask4[:], OP.mult),
                        reads=[psb, bw["mask"]], writes=[bscores[wslot]])

                    def fat(e, w=w, wslot=wslot):
                        r = None
                        for hw in range(4):
                            h = 4 * w + hw
                            r = e.matmul(pbC[:, hw * 65:hw * 65 + 65],
                                         scores_sb[:, wslot, hw, :],
                                         vs_sb[:, h, 0:65],
                                         start=(hw == 0), stop=False)
                            for mc in range(2):
                                r = e.matmul(pbC[:, hw * 65:hw * 65 + 65],
                                             qp_sb[:, h, mc, :],
                                             state_sb[:, h, mc, 0:65],
                                             start=False,
                                             stop=(hw == 3 and mc == 1))
                        return r
                    P.op("pe", fat, reads=[bscores[wslot], bvs, bqp, bstate_sb],
                         writes=[bC])

                    def fup(e, w=w, sub_g=sub_g):
                        r = None
                        for hw in range(4):
                            h = 4 * w + hw
                            for mc in range(2):
                                u = 2 * h + mc
                                off = 512 * (u // 8) + 64 * (u % 8)
                                st = (sub_g == 0 and (u % 8) == 0)
                                r = e.matmul(state_ps[:, off:off + 64],
                                             kpt_sb.ap()[:, h,
                                                         mc * 128:(mc + 1) * 128],
                                             vs_sb[:, h, 0:64],
                                             start=st, stop=False,
                                             skip_group_check=True)
                        return r
                    P.op("pe", fup, reads=[bkpt, bvs], writes=[bst_ps])
                    dens = bass.AP(pbC.ap().tensor, 64, [[512, 128], [65, 4]])
                    P.op("dve", lambda e, w=w, dens=dens: e.reciprocal(
                        recden[:, 4 * w:4 * w + 4], dens),
                        reads=[bC], writes=[brec], small=True)
                    nums = bass.AP(pbC.ap().tensor, 0,
                                   [[512, 128], [65, 4], [1, 64]])
                    P.op("dve", lambda e, w=w, nums=nums: e.tensor_tensor(
                        attn_tm[:, 4 * w * 64:(4 * w + 4) * 64], nums,
                        bass.AP(recden.ap().tensor, 4 * w,
                                [[16, 128], [1, 4], [0, 64]]),
                        OP.mult),
                        reads=[bC, brec], writes=[battn_tm])

                st_src = bass.AP(state_ps.ap().tensor, 0,
                                 [[2048, 128], [64, 32], [1, 64]])
                st_dst = bass.AP(state_sb.ap().tensor, 0,
                                 [[16 * 2 * 66, 128], [66, 32], [1, 64]])
                P.op("act", lambda e, st_src=st_src, st_dst=st_dst: e.copy(
                    st_dst, st_src),
                    reads=[bst_ps], writes=[bstate_sb])
                P.op("dve", lambda e: e.tensor_copy(
                    bass.AP(state_sb.ap().tensor, 64,
                            [[16 * 2 * 66, 128], [66, 32], [1, 1]]),
                    ksum_sb[:, 0:32]),
                    reads=[bksum], writes=[bstate_sb], small=True)

                def fatt(e):
                    for ec in range(8):
                        e.transpose(pbT[:, ec * 128:(ec + 1) * 128],
                                    attn_tm.ap()[:, ec * 128:(ec + 1) * 128],
                                    ident[:])
                    return e.drain()
                P.op("pe", fatt, reads=[battn_tm, bw["ident"]], writes=[bT])
                dstT = bass.AP(attnT.ap().tensor, sj * 128,
                               [[8 * CO, 128], [CO, 8], [1, 128]]).bitcast(I32)
                P.op("dve", lambda e, dstT=dstT: e.tensor_copy(dstT, pbT.ap().bitcast(I32)),
                     reads=[bT], writes=[battnT])

            for j in range(4):
                P.op("dve", lambda e, j=j: e.tensor_scalar(
                    res_t[:, j, :], xc_t[:, j, :], nmu[:, j:j + 1],
                    rstd_sb[:, j:j + 1], OP.add, OP.mult),
                    reads=[xc_buf, bagg, brstd], writes=[res_buf])
            for sj in range(4):
                for hf in range(2):
                    pb, pbb = next_pb()

                    def fow(e, pb=pb, sj=sj, hf=hf):
                        for ei in range(8):
                            e.matmul(pb[:, 0:512],
                                     attnT[:, ei, sj * 128:(sj + 1) * 128],
                                     wo_sb[:, ei, hf * 512:(hf + 1) * 512],
                                     start=(ei == 0), stop=False)
                        return e.matmul(
                            pb[:, 0:512], ones_row[:],
                            rows_bf.ap()[0:1, OB + hf * 512:OB + (hf + 1) * 512],
                            start=False, stop=True)
                    P.op("pe", fow, reads=[battnT, bw["wo"], bw["ones"], bw["rows"]],
                         writes=[pbb])
                    P.op("dve", lambda e, pb=pb, sj=sj, hf=hf:
                         e.scalar_tensor_tensor(
                             res_t[:, sj, hf * 512:(hf + 1) * 512], pb[:, 0:512],
                             0.0, res_t[:, sj, hf * 512:(hf + 1) * 512],
                             OP.add, OP.add),
                         reads=[pbb], writes=[res_buf])
            nmu = ln_full(res_t, res_buf, eps6, bw["eps6"])
            for j in range(4):
                P.op("dve", lambda e, j=j: e.tensor_scalar(
                    res_t[:, j, :], res_t[:, j, :], nmu[:, j:j + 1],
                    rstd_sb[:, j:j + 1], OP.add, OP.mult),
                    reads=[bagg, brstd], writes=[res_buf])
                P.op("dve", lambda e, j=j: e.tensor_tensor(
                    res_t[:, j, :], res_t[:, j, :], gam_rep[:], OP.mult),
                    reads=[bw["gam"]], writes=[res_buf])
                P.op("dve", lambda e, j=j: e.tensor_tensor(
                    xc_t[:, j, :], xc_t[:, j, :], res_t[:, j, :], OP.add),
                    reads=[res_buf], writes=[xc_buf])
                P.op("dve", lambda e, j=j: e.tensor_tensor(
                    xc_t[:, j, :], xc_t[:, j, :], bet_rep[:], OP.add),
                    reads=[bw["bet"]], writes=[xc_buf])

        P.sp_chain = False
        for c in range(n_chunks):
            cur = c % 2
            xc_t, xc_buf = xres[cur], bx[cur]
            res_t, res_buf = xres[1 - cur], bx[1 - cur]
            xsem = "x0" if c % 2 == 0 else "x1"
            xsh = P.sems[xsem].h

            def fload(e, c=c, xc_t=xc_t, xsh=xsh):
                for j in range(4):
                    src = bass.AP(d_x.tensor, (c * CO + j * 128) * E,
                                  [[E, 128], [1, E]])
                    e.dma_start(out=xc_t[:, j, :], in_=src).then_inc(xsh, 16)
                return None
            P.op("sp", fload, writes=[xc_buf], dma=True, dma_n=4, sem=xsem)
            ffn_sublayer(xc_t, xc_buf)
            attn_sublayer(c, xc_t, xc_buf, res_t, res_buf)
            ffn_sublayer(xc_t, xc_buf)
            ssem = "s0" if c % 2 == 0 else "s1"
            ssh = P.sems[ssem].h

            def fstore(e, c=c, xc_t=xc_t, ssh=ssh):
                for j in range(4):
                    dstm = bass.AP(d_y.tensor, (c * CO + j * 128) * E,
                                   [[E, 128], [1, E]])
                    e.dma_start(out=dstm, in_=xc_t[:, j, :]).then_inc(ssh, 16)
                return None
            P.op("sp", fstore, reads=[xc_buf], dma=True, dma_n=4, sem=ssem)

        if max_ops is not None:
            P.ops = P.ops[:max_ops]
        if dbg is not None:
            dbg_t = {"qfm": qfm_sb, "kfm": kfm_sb, "cx": cx_sb, "qp": qp_sb,
                     "kpt": kpt_sb, "kpf": kpf_sb, "scores": scores_sb,
                     "state": state_sb, "attn_tm": attn_tm, "attnT": attnT,
                     "v": v_sb, "vs": vs_sb, "xlnfm": xlnfm, "xlnst": xln_st,
                     "ksum": ksum_sb, "s": s_sb, "g": g_sb, "recden": recden,
                     "agg": agg_sb, "rstd": rstd_sb, "projz": projz, "stats": stats_sb,
                     "w1e": w1e_sb, "wout": wout_sb, "gam": gam_rep,
                     "xres0": xres[0], "xres1": xres[1]}[dbg]
            shp = list(dbg_t.ap().shape)
            n_free = 1
            for s_ in shp[1:]:
                n_free *= s_
            d_dbg = nc.dram_tensor("dbg_out", [128, n_free], F32,
                                   kind="ExternalOutput").ap()
            _tot = {}
            for (_e2, _f2, _w2, (semh2, amt2)) in P.ops:
                _tot[semh2] = _tot.get(semh2, 0) + amt2
            extras = [(P.sems[nm], _tot.get(P.sems[nm].h, 0)) for nm in
                      ("pe", "dve", "act", "gp")]
            flat = bass.AP(dbg_t.ap().tensor, 0, [[n_free, 128], [1, n_free]])
            P.op("gp", lambda e: e.dma_start(out=d_dbg, in_=flat),
                 extra=extras, dma=True, sem="sp")

        totals = {}
        for (_e, _f, _w, (semh, amt)) in P.ops:
            totals[semh] = totals.get(semh, 0) + amt
        finals = [(P.sems[nm].h, totals.get(P.sems[nm].h, 0))
                  for nm in ("sp", "x0", "x1", "s0", "s1")
                  if totals.get(P.sems[nm].h, 0) > 0]

        with nc.Block() as block:
            @block.sync
            def _(e):
                P.emit("sp", e, self_waits=sim_races)
                for (h, v) in finals:
                    e.wait_ge(h, v)

            @block.vector
            def _(e):
                P.emit("dve", e, self_waits=sim_races)

            @block.scalar
            def _(e):
                P.emit("act", e, self_waits=sim_races)

            @block.tensor
            def _(e):
                P.emit("pe", e, self_waits=sim_races)

            @block.gpsimd
            def _(e):
                P.emit("gp", e, self_waits=sim_races)

    return nc


def kernel(**inputs):
    x = np.ascontiguousarray(inputs["L0_attn_x"])
    nc = build_program()
    common = {
        "q_w": inputs["L0_attn_q_w"], "q_b": inputs["L0_attn_q_b"],
        "k_w": inputs["L0_attn_k_w"], "k_b": inputs["L0_attn_k_b"],
        "v_w": inputs["L0_attn_v_w"], "v_b": inputs["L0_attn_v_b"],
        "out_w": inputs["L0_attn_out_w"], "out_b": inputs["L0_attn_out_b"],
        "proj": inputs["L0_attn_proj"], "gamma": inputs["L0_attn_gamma"],
        "beta": inputs["L0_attn_beta"],
        "Win": inputs["L0_ffn_Win"], "bin_": inputs["L0_ffn_bin_"],
        "W1": inputs["L0_ffn_W1"], "b1": inputs["L0_ffn_b1"],
        "W2": inputs["L0_ffn_W2"], "b2": inputs["L0_ffn_b2"],
        "Wout": inputs["L0_ffn_Wout"], "bout": inputs["L0_ffn_bout"],
    }
    common = {k: np.ascontiguousarray(v, dtype=np.float32) for k, v in common.items()}
    in_maps = []
    for b in range(8):
        m = dict(common)
        m["x_in"] = np.ascontiguousarray(x[b])
        in_maps.append(m)
    res = run_bass_kernel_spmd(nc, in_maps, core_ids=list(range(8)))
    out = np.stack([r["x_out"] for r in res.results], axis=0).astype(np.float32)
    return out


# version 06 (final)
# speedup vs baseline: 1.0387x; 1.0387x over previous
import numpy as np
from contextlib import ExitStack

import concourse.bass as bass
import concourse.mybir as mybir
from concourse.bass_utils import run_bass_kernel_spmd

F32 = mybir.dt.float32
I32 = mybir.dt.int32
BF16 = mybir.dt.bfloat16
AF = mybir.ActivationFunctionType
OP = mybir.AluOpType
AX = mybir.AxisListType

S, E, H, DH, M = 4096, 1024, 16, 64, 256
CO, CS = 512, 128
NCH, NSUB = S // CO, CO // CS
LN_EPS = 1e-5
ATT_EPS = 1e-6


class Sem:
    def __init__(self, h):
        self.h = h
        self.n = 0


class Buf:
    def __init__(self, name=""):
        self.name = name
        self.w = None
        self.readers = []


class Prog:
    def __init__(self):
        self.ops = []
        self.sems = {}
        self.sp_chain = True
        self._last_sp = None
        self.small_toks = set()

    def add_sem(self, name, handle):
        self.sems[name] = Sem(handle)

    def op(self, eng, fn, reads=(), writes=(), extra=(), dma=False, dma_n=1,
           sem=None, small=False):
        sem = self.sems[sem or eng]
        waits = []
        for b in reads:
            if b.w is not None:
                waits.append(b.w)
        for b in writes:
            if b.w is not None:
                waits.append(b.w)
            waits.extend(b.readers)
        waits.extend([t for t in extra if t is not None])
        amt = (16 * dma_n) if dma else 1
        if eng == "sp" and self.sp_chain and self._last_sp is not None:
            waits.append(self._last_sp)
        sem.n += amt
        tok = (sem, sem.n)
        if small:
            self.small_toks.add((id(sem), sem.n))
        if eng == "sp":
            self._last_sp = tok
        self.ops.append((eng, fn, tuple(waits), (sem.h, amt)))
        for b in reads:
            b.readers.append(tok)
        for b in writes:
            b.w = tok
            b.readers = []
        return tok

    def emit(self, name, eng, self_waits=False):
        last_wait = {}
        my_sem = self.sems[name]
        skip_self = (name != "sp") and not self_waits
        guard = name in ("dve", "act", "gp")
        own_val = 0
        for i, (e, fn, waits, (semh, amt)) in enumerate(self.ops):
            if e != name:
                continue
            near_v = -1
            for (sem, v) in waits:
                if sem is my_sem:
                    if guard and v >= own_val - 1 and (id(sem), v) in self.small_toks:
                        near_v = max(near_v, v)
                    if skip_self:
                        continue
                if last_wait.get(id(sem), -1) >= v:
                    continue
                last_wait[id(sem)] = v
                eng.wait_ge(sem.h, v)
            if near_v >= 0:
                eng.wait_ge(my_sem.h, near_v)
            inst = fn(eng)
            if inst is not None:
                inst.then_inc(semh, amt)
            own_val += amt


def build_program(n_chunks=NCH, sim_races=False, max_ops=None, dbg=None):
    nc = bass.Bass("TRN2", target_bir_lowering=False, debug=False,
                   detect_race_conditions=sim_races)

    d_x = nc.dram_tensor("x_in", [S, E], F32, kind="ExternalInput").ap()
    d_qw = nc.dram_tensor("q_w", [E, E], F32, kind="ExternalInput").ap()
    d_qb = nc.dram_tensor("q_b", [E], F32, kind="ExternalInput").ap()
    d_kw = nc.dram_tensor("k_w", [E, E], F32, kind="ExternalInput").ap()
    d_kb = nc.dram_tensor("k_b", [E], F32, kind="ExternalInput").ap()
    d_vw = nc.dram_tensor("v_w", [E, E], F32, kind="ExternalInput").ap()
    d_vb = nc.dram_tensor("v_b", [E], F32, kind="ExternalInput").ap()
    d_ow = nc.dram_tensor("out_w", [E, E], F32, kind="ExternalInput").ap()
    d_ob = nc.dram_tensor("out_b", [E], F32, kind="ExternalInput").ap()
    d_proj = nc.dram_tensor("proj", [M, DH], F32, kind="ExternalInput").ap()
    d_gam = nc.dram_tensor("gamma", [E], F32, kind="ExternalInput").ap()
    d_bet = nc.dram_tensor("beta", [E], F32, kind="ExternalInput").ap()
    d_win = nc.dram_tensor("Win", [E, 64], F32, kind="ExternalInput").ap()
    d_bin = nc.dram_tensor("bin_", [64], F32, kind="ExternalInput").ap()
    d_w1 = nc.dram_tensor("W1", [64, 16], F32, kind="ExternalInput").ap()
    d_b1 = nc.dram_tensor("b1", [64, 16], F32, kind="ExternalInput").ap()
    d_w2 = nc.dram_tensor("W2", [64, 16], F32, kind="ExternalInput").ap()
    d_b2 = nc.dram_tensor("b2", [64], F32, kind="ExternalInput").ap()
    d_wout = nc.dram_tensor("Wout", [64, E], F32, kind="ExternalInput").ap()
    d_bout = nc.dram_tensor("bout", [E], F32, kind="ExternalInput").ap()
    d_y = nc.dram_tensor("x_out", [S, E], F32, kind="ExternalOutput").ap()

    P = Prog()

    with ExitStack() as ctx:
        En = ctx.enter_context

        def sb(name, shape, dt):
            return En(nc.sbuf_tensor(name, shape, dt))

        xres = [sb("xres0", [128, 4, E], F32), sb("xres1", [128, 4, E], F32)]
        wq_sb = sb("wq", [128, 8, E], BF16)
        wk_sb = sb("wk", [128, 8, E], BF16)
        wv_sb = sb("wv", [128, 8, E], BF16)
        wo_sb = sb("wo", [128, 8, E], BF16)
        win_sb = sb("win", [128, 8, 64], BF16)
        w1e_sb = sb("w1e", [128, E], BF16)
        w2e_sb = sb("w2e", [128, 8, 64], BF16)
        wout_sb = sb("wout", [128, E], BF16)
        projz = sb("projz", [128, 2, M], BF16)
        gam_rep = sb("gam_rep", [128, E], F32)
        bet_rep = sb("bet_rep", [128, E], F32)
        rows_bf = sb("rows_bf", [1, 3 * E], BF16)
        ones_row = sb("ones_row", [1, 128], BF16)
        onesf_row = sb("onesf_row", [1, 128], F32)
        qb_fm = sb("qb_fm", [128, 8], F32)
        kb_fm = sb("kb_fm", [128, 8], F32)
        bin_sb = sb("binv", [64, 1], F32)
        b1f_sb = sb("b1f", [128, 8], F32)
        b2_sb = sb("b2v", [64, 1], F32)
        w2f_sb = sb("w2f", [128, 8], F32)
        eps5 = sb("eps5", [128, 1], F32)
        eps6 = sb("eps6", [128, 1], F32)
        ident = sb("ident", [128, 128], BF16)
        mask4 = sb("mask4", [128, 4, 128], BF16)
        ind_sb = sb("ind", [128, 8, 16], BF16)
        xln_st = sb("xlnst", [128, E], BF16)
        xlnfm = sb("xlnfm", [128, 8, CO], BF16)
        qfm_sb = sb("qfm", [128, 8, CO], BF16)
        kfm_sb = sb("kfm", [128, 8, CO], BF16)
        v_sb = sb("v", [128, 16, 66], BF16)
        vs_sb = sb("vs", [128, 16, 66], BF16)
        cx_sb = sb("cx", [128, 4, 16], BF16)
        qp_sb = sb("qp", [128, 16, 2, 128], BF16)
        kpf_sb = sb("kpf", [128, 16, 2, 128], BF16)
        kpt_sb = sb("kpt", [128, 16, M], BF16)
        state_sb = sb("state", [128, 16, 2, 66], BF16)
        ksum_sb = sb("ksum", [128, 32], F32)
        scores_sb = sb("scores", [128, 2, 4, 128], BF16)
        attn_tm = sb("attn_tm", [128, 16 * 64], BF16)
        attnT = sb("attnT", [128, 8, CO], BF16)
        s_sb = sb("s_fm", [128, CO], BF16)
        h_sb = sb("h_fm", [128, 2, CO], BF16)
        g_sb = sb("g_fm", [128, CO], BF16)
        stats_sb = sb("stats", [128, 4, 12], F32)
        agg_sb = sb("agg", [128, 24], F32)
        rstd_sb = sb("rstd", [128, 4], F32)
        recden = sb("recden", [128, 16], F32)

        state_ps = En(nc.psum_tensor("state_ps", [128, 2048], F32))
        pbA = En(nc.psum_tensor("pbA", [128, 512], F32))
        pbB = En(nc.psum_tensor("pbB", [128, 512], F32))
        pbC = En(nc.psum_tensor("pbC", [128, 512], F32))
        pbT = En(nc.psum_tensor("pbT", [128, 1024], BF16))

        for nm in ("sp", "pe", "dve", "act", "gp", "x0", "x1", "s0", "s1"):
            P.add_sem(nm, En(nc.semaphore("sem_" + nm)))

        B = lambda n: Buf(n)
        bx = [B("x0"), B("x1")]
        bw = {k: B(k) for k in ["wq", "wk", "wv", "wo", "win", "w1e", "w2e",
                                "wout", "projTd", "gam", "bet", "rows", "ones",
                                "qb", "kb", "bin", "b1f", "b2", "w2f", "eps5",
                                "eps6", "ident", "mask", "ind"]}
        bstage = [B("st0"), B("st1"), B("st2"), B("st3")]
        bxlnst = B("xlnst")
        bxlnfm = B("xlnfm")
        bqfm, bkfm = B("qfm"), B("kfm")
        bv = B("v")
        bvs = B("vs")
        bcx = B("cx")
        bqp, bkpf, bkpt = B("qp"), B("kpf"), B("kpt")
        bstate_sb, bksum = B("state_sb"), B("ksum")
        bscores = [B("sc0"), B("sc1")]
        battn_tm, battnT = B("attn_tm"), B("attnT")
        bs_sb, bh, bg = B("s"), B("h"), B("g")
        bstats, bagg, brstd = B("stats"), B("agg"), B("rstd")
        brec = B("recden")
        bA, bB_, bC, bT = B("pbA"), B("pbB"), B("pbC"), B("pbT")
        bst_ps = B("state_ps")

        stage = xres[0]
        scr_t = xres[1]
        bscr = bx[1]

        P.op("dve", lambda e: e.memset(ones_row[:], 1.0), writes=[bw["ones"]])
        P.op("dve", lambda e: e.memset(onesf_row[:], 1.0), writes=[bw["ones"]])
        P.op("dve", lambda e: e.memset(eps5[:], LN_EPS), writes=[bw["eps5"]])
        P.op("dve", lambda e: e.memset(eps6[:], ATT_EPS), writes=[bw["eps6"]])
        P.op("dve", lambda e: e.memset(ksum_sb[:], 0.0), writes=[bksum])
        P.op("dve", lambda e: e.memset(state_sb[:], 0.0), writes=[bstate_sb])
        P.op("dve", lambda e: e.memset(projz[:], 0.0), writes=[bw["projTd"]])
        P.op("dve", lambda e: e.memset(s_sb.ap()[64:128, :], 0.0), writes=[bs_sb])
        P.op("dve", lambda e: e.memset(g_sb.ap()[64:128, :], 0.0), writes=[bg])
        P.op("dve", lambda e: e.memset(w1e_sb.ap()[64:128, :], 0.0), writes=[bw["w1e"]])
        P.op("dve", lambda e: e.memset(wout_sb.ap()[64:128, :], 0.0), writes=[bw["wout"]])

        P.op("gp", lambda e: e.iota(scr_t[:, 3, 0:128], [[1, 128]], base=0,
                                    channel_multiplier=-1,
                                    allow_small_or_imprecise_dtypes=True),
             writes=[bscr])
        P.op("dve", lambda e: e.tensor_scalar(mask4[:, 0, :], scr_t[:, 3, 0:128],
                                              0.0, None, OP.is_ge),
             reads=[bscr], writes=[bw["mask"]])
        P.op("dve", lambda e: e.tensor_scalar(ident[:], scr_t[:, 3, 0:128],
                                              0.0, None, OP.is_equal),
             reads=[bscr], writes=[bw["ident"]])
        for r in range(1, 4):
            P.op("dve", lambda e, r=r: e.tensor_copy(mask4[:, r, :], mask4[:, 0, :]),
                 reads=[bw["mask"]], writes=[bw["mask"]])
        P.op("dve", lambda e: e.memset(ind_sb[:], 0.0), writes=[bw["ind"]])
        for ei in range(8):
            P.op("dve", lambda e, ei=ei: e.memset(
                ind_sb.ap()[0:64, ei, 2 * ei:2 * ei + 1], 1.0), writes=[bw["ind"]])
            P.op("dve", lambda e, ei=ei: e.memset(
                ind_sb.ap()[64:128, ei, 2 * ei + 1:2 * ei + 2], 1.0), writes=[bw["ind"]])

        def load_big(dsrc, dst_sb, dstbuf):
            for ei in range(8):
                slot = ei % 4
                src = bass.AP(dsrc.tensor, ei * 128 * E, [[E, 128], [1, E]])
                P.op("sp", lambda e, src=src, slot=slot: e.dma_start(
                    out=stage[:, slot, :], in_=src), writes=[bstage[slot]],
                    dma=True)
                if ei % 2 == 0:
                    P.op("dve", lambda e, slot=slot, ei=ei: e.tensor_copy(
                        dst_sb[:, ei, :], stage[:, slot, :]),
                        reads=[bstage[slot]], writes=[dstbuf])
                else:
                    P.op("act", lambda e, slot=slot, ei=ei: e.copy(
                        dst_sb[:, ei, :], stage[:, slot, :]),
                        reads=[bstage[slot]], writes=[dstbuf])

        load_big(d_qw, wq_sb, bw["wq"])
        load_big(d_kw, wk_sb, bw["wk"])
        load_big(d_vw, wv_sb, bw["wv"])
        load_big(d_ow, wo_sb, bw["wo"])

        for ei in range(8):
            src = bass.AP(d_win.tensor, ei * 128 * 64, [[64, 128], [1, 64]])
            P.op("sp", lambda e, src=src: e.dma_start(out=stage[:, 0, 0:64], in_=src),
                 writes=[bstage[0]], dma=True)
            P.op("dve", lambda e, ei=ei: e.tensor_copy(win_sb[:, ei, :],
                                                       stage[:, 0, 0:64]),
                 reads=[bstage[0]], writes=[bw["win"]])
        src = bass.AP(d_wout.tensor, 0, [[E, 64], [1, E]])
        P.op("sp", lambda e, src=src: e.dma_start(out=stage.ap()[0:64, 0, :], in_=src),
             writes=[bstage[0]], dma=True)
        P.op("dve", lambda e: e.tensor_copy(wout_sb.ap()[0:64, :],
                                            stage.ap()[0:64, 0, :]),
             reads=[bstage[0]], writes=[bw["wout"]])

        def ld_small(dsrc, dst_ap, bufw, apdesc):
            src = bass.AP(dsrc.tensor, 0, apdesc)
            P.op("sp", lambda e, src=src, dst_ap=dst_ap: e.dma_start(
                out=dst_ap, in_=src, allow_slow_non_contiguous=True),
                writes=[bufw], dma=True)

        ld_small(d_qb, qb_fm[:], bw["qb"], [[1, 128], [128, 8]])
        ld_small(d_kb, kb_fm[:], bw["kb"], [[1, 128], [128, 8]])
        ld_small(d_bin, bin_sb[:], bw["bin"], [[1, 64], [1, 1]])
        ld_small(d_b1, b1f_sb[:], bw["b1f"], [[1, 128], [128, 8]])
        ld_small(d_b2, b2_sb[:], bw["b2"], [[1, 64], [1, 1]])
        ld_small(d_w2, w2f_sb[:], bw["w2f"], [[1, 128], [128, 8]])

        def ld_row_to(dsrc, dst_row_ap, n=E):
            src = bass.AP(dsrc.tensor, 0, [[n, 1], [1, n]])
            P.op("sp", lambda e, src=src, n=n: e.dma_start(
                out=scr_t.ap()[0:1, 3, 0:n], in_=src), writes=[bscr], dma=True)
            P.op("dve", lambda e, dst_row_ap=dst_row_ap, n=n: e.tensor_copy(
                dst_row_ap, scr_t.ap()[0:1, 3, 0:n]),
                reads=[bscr], writes=[bw["rows"]])

        def bcast_row_f32(dsrc, dst_sb, dstbuf):
            srcap = bass.AP(dsrc.tensor, 0, [[E, 1], [1, E]])
            P.op("sp", lambda e, srcap=srcap: e.dma_start(
                out=scr_t.ap()[0:1, 3, 0:E], in_=srcap), writes=[bscr], dma=True)
            for hf in range(2):
                pb = pbA if hf == 0 else pbB
                pbb = bA if hf == 0 else bB_
                P.op("pe", lambda e, hf=hf, pb=pb: e.matmul(
                    pb[:, 0:512], onesf_row[:],
                    scr_t.ap()[0:1, 3, hf * 512:(hf + 1) * 512],
                    start=True, stop=True),
                    reads=[bscr, bw["ones"]], writes=[pbb])
                P.op("dve", lambda e, hf=hf, pb=pb, dst_sb=dst_sb: e.tensor_copy(
                    dst_sb[:, hf * 512:(hf + 1) * 512], pb[:, 0:512]),
                    reads=[bA if hf == 0 else bB_], writes=[dstbuf])

        bcast_row_f32(d_gam, gam_rep, bw["gam"])
        bcast_row_f32(d_bet, bet_rep, bw["bet"])

        ld_row_to(d_w1, rows_bf.ap()[0:1, 0:E], n=1024)
        for hf in range(2):
            pb = pbA if hf == 0 else pbB
            pbb = bA if hf == 0 else bB_
            P.op("pe", lambda e, hf=hf, pb=pb: e.matmul(
                pb.ap()[0:64, 0:512], ones_row.ap()[0:1, 0:64],
                rows_bf.ap()[0:1, hf * 512:(hf + 1) * 512],
                start=True, stop=True),
                reads=[bw["rows"], bw["ones"]], writes=[pbb])
        P.op("gp", lambda e: e.iota(scr_t.ap()[0:64, 0, :], [[1, 1024]], base=0,
                                    channel_multiplier=-16,
                                    allow_small_or_imprecise_dtypes=True),
             writes=[bscr])
        P.op("dve", lambda e: e.tensor_scalar(scr_t.ap()[0:64, 1, :],
                                              scr_t.ap()[0:64, 0, :],
                                              0.0, None, OP.is_ge),
             writes=[bscr])
        P.op("dve", lambda e: e.tensor_scalar(scr_t.ap()[0:64, 2, :],
                                              scr_t.ap()[0:64, 0, :],
                                              15.0, None, OP.is_le),
             writes=[bscr])
        P.op("dve", lambda e: e.tensor_tensor(scr_t.ap()[0:64, 1, :],
                                              scr_t.ap()[0:64, 1, :],
                                              scr_t.ap()[0:64, 2, :], OP.mult),
             writes=[bscr])
        for hf in range(2):
            pb = pbA if hf == 0 else pbB
            pbb = bA if hf == 0 else bB_
            P.op("dve", lambda e, hf=hf, pb=pb: e.tensor_tensor(
                w1e_sb.ap()[0:64, hf * 512:(hf + 1) * 512],
                scr_t.ap()[0:64, 1, hf * 512:(hf + 1) * 512],
                pb.ap()[0:64, 0:512], OP.mult),
                reads=[bscr, bA if hf == 0 else bB_], writes=[bw["w1e"]])

        P.op("gp", lambda e: e.iota(scr_t[:, 3, 256:320], [[16, 64]], base=0,
                                    channel_multiplier=-1,
                                    allow_small_or_imprecise_dtypes=True),
             writes=[bscr])
        for jt in range(8):
            P.op("dve", lambda e: e.memset(scr_t[:, 3, 0:64], 1.0), writes=[bscr])
            P.op("dve", lambda e, jt=jt: e.tensor_scalar(
                scr_t[:, 3, 64:128], scr_t[:, 3, 0:64], w2f_sb[:, jt:jt + 1],
                None, OP.mult),
                reads=[bw["w2f"]], writes=[bscr])
            P.op("dve", lambda e, jt=jt: e.tensor_scalar(
                scr_t[:, 3, 128:192], scr_t[:, 3, 256:320],
                float(128 * jt - 15), None, OP.is_ge),
                writes=[bscr])
            P.op("dve", lambda e, jt=jt: e.tensor_scalar(
                scr_t[:, 3, 192:256], scr_t[:, 3, 256:320],
                float(128 * jt), None, OP.is_le),
                writes=[bscr])
            P.op("dve", lambda e, jt=jt: e.tensor_tensor(
                scr_t[:, 3, 128:192], scr_t[:, 3, 128:192], scr_t[:, 3, 192:256],
                OP.mult),
                writes=[bscr])
            P.op("dve", lambda e, jt=jt: e.tensor_tensor(
                w2e_sb[:, jt, :], scr_t[:, 3, 128:192], scr_t[:, 3, 64:128],
                OP.mult),
                reads=[bscr], writes=[bw["w2e"]])

        for mt in range(2):
            src = bass.AP(d_proj.tensor, mt * 128 * 64, [[64, 128], [1, 64]])
            P.op("sp", lambda e, src=src: e.dma_start(out=scr_t[:, 3, 0:64], in_=src),
                 writes=[bscr], dma=True)
            P.op("dve", lambda e: e.tensor_copy(attn_tm.ap()[:, 0:64],
                                                scr_t[:, 3, 0:64]),
                 reads=[bscr], writes=[battn_tm])
            def ftp(e, mt=mt):
                e.transpose(pbT.ap()[0:64, mt * 128:(mt + 1) * 128],
                            attn_tm.ap()[:, 0:64], ident[:])
                return e.drain()
            P.op("pe", ftp, reads=[battn_tm, bw["ident"]], writes=[bT])
        P.op("dve", lambda e: e.tensor_copy(projz.ap()[0:64, 0, :].bitcast(I32),
                                            pbT.ap()[0:64, 0:256].bitcast(I32)),
             reads=[bT], writes=[bw["projTd"]])
        P.op("sp", lambda e: e.dma_start(out=projz.ap()[64:128, 1, :],
                                         in_=projz.ap()[0:64, 0, :]),
             reads=[bw["projTd"]], writes=[bw["projTd"]], dma=True)

        ld_row_to(d_vb, rows_bf.ap()[0:1, 0:E])
        ld_row_to(d_ob, rows_bf.ap()[0:1, E:2 * E])
        ld_row_to(d_bout, rows_bf.ap()[0:1, 2 * E:3 * E])
        VB, OB, BO = 0, E, 2 * E

        pb_rot = [(pbA, bA), (pbB, bB_), (pbC, bC)]
        rot_i = [0]

        def next_pb():
            pb, pbb = pb_rot[rot_i[0] % 3]
            rot_i[0] += 1
            return pb, pbb

        def ln_full(src_t, src_buf, eps_t, eps_b):
            def f(e):
                r = None
                for j in range(4):
                    r = e.bn_stats(stats_sb[:, j, 0:6], src_t[:, j, 0:512])
                    r = e.bn_stats(stats_sb[:, j, 6:12], src_t[:, j, 512:1024])
                return r
            P.op("dve", f, reads=[src_buf], writes=[bstats])
            means = bass.AP(stats_sb.ap().tensor, 1, [[48, 128], [12, 4], [3, 4]])
            m2s = bass.AP(stats_sb.ap().tensor, 2, [[48, 128], [12, 4], [3, 4]])
            s1 = agg_sb[:, 0:4]
            s2 = agg_sb[:, 4:8]
            s3 = agg_sb[:, 8:12]
            nmu = agg_sb[:, 12:16]
            var = agg_sb[:, 16:20]

            stt = stats_sb.ap().tensor

            def g1(e):
                r = None
                for j in range(4):
                    e.tensor_reduce(s1[:, j:j + 1],
                                    bass.AP(stt, 12 * j + 1, [[48, 128], [3, 4]]),
                                    AX.X, OP.add)
                    r = e.tensor_reduce(s2[:, j:j + 1],
                                        bass.AP(stt, 12 * j + 2, [[48, 128], [3, 4]]),
                                        AX.X, OP.add)
                return r
            P.op("dve", g1, reads=[bstats], writes=[bagg], small=True)
            P.op("dve", lambda e: e.tensor_tensor(recden[:], means, means, OP.mult),
                 reads=[bstats], writes=[brec], small=True)

            def g2(e):
                r = None
                for j in range(4):
                    r = e.tensor_reduce(s3[:, j:j + 1],
                                        recden[:, 4 * j:4 * j + 4], AX.X, OP.add)
                return r
            P.op("dve", g2, reads=[brec], writes=[bagg], small=True)
            P.op("dve", lambda e: e.tensor_scalar(nmu, s1, -0.25, None, OP.mult),
                 writes=[bagg], small=True)
            P.op("dve", lambda e: e.scalar_tensor_tensor(
                agg_sb[:, 20:24], s3, 256.0, s2, OP.mult, OP.add),
                writes=[bagg], small=True)
            P.op("dve", lambda e: e.tensor_tensor(s3, nmu, nmu, OP.mult),
                 writes=[bagg], small=True)
            P.op("dve", lambda e: e.tensor_scalar(var, agg_sb[:, 20:24],
                                                  1.0 / 1024.0, None, OP.mult),
                 writes=[bagg], small=True)
            P.op("dve", lambda e: e.tensor_tensor(var, var, s3, OP.subtract),
                 writes=[bagg], small=True)
            P.op("act", lambda e: e.activation(rstd_sb[:], var, AF.Sqrt,
                                               bias=eps_t[:], scale=1.0),
                 reads=[bagg, eps_b], writes=[brstd], small=True)
            P.op("dve", lambda e: e.reciprocal(rstd_sb[:], rstd_sb[:]),
                 writes=[brstd], small=True)
            return nmu

        def ln_apply_j(src_t, src_buf, nmu, j, out_ap, out_buf):
            P.op("dve", lambda e, j=j, out_ap=out_ap: e.tensor_scalar(
                out_ap, src_t[:, j, :], nmu[:, j:j + 1], rstd_sb[:, j:j + 1],
                OP.add, OP.mult),
                reads=[src_buf, bagg, brstd], writes=[out_buf])

        def transpose_to_fm(src_ap, src_buf, j):
            def f(e, src_ap=src_ap):
                for ec in range(8):
                    e.transpose(pbT[:, ec * 128:(ec + 1) * 128],
                                src_ap[:, ec * 128:(ec + 1) * 128], ident[:])
                return e.drain()
            P.op("pe", f, reads=[src_buf, bw["ident"]], writes=[bT])
            dst = bass.AP(xlnfm.ap().tensor, j * 128,
                          [[8 * CO, 128], [CO, 8], [1, 128]]).bitcast(I32)
            P.op("dve", lambda e, dst=dst: e.tensor_copy(dst, pbT.ap().bitcast(I32)),
                 reads=[bT], writes=[bxlnfm])

        def ffn_sublayer(xc_t, xc_buf):
            nmu = ln_full(xc_t, xc_buf, eps5, bw["eps5"])
            for j in range(4):
                ln_apply_j(xc_t, xc_buf, nmu, j, xln_st[:], bxlnst)
                transpose_to_fm(xln_st.ap(), bxlnst, j)
            pb, pbb = next_pb()

            def fs(e, pb=pb):
                r = None
                for ei in range(8):
                    r = e.matmul(pb.ap()[0:64, 0:512], win_sb[:, ei, :],
                                 xlnfm[:, ei, :], start=(ei == 0), stop=(ei == 7))
                return r
            P.op("pe", fs, reads=[bxlnfm, bw["win"]], writes=[pbb])
            P.op("act", lambda e, pb=pb: e.activation(
                s_sb.ap()[0:64, :], pb.ap()[0:64, 0:512], AF.Identity,
                bias=bin_sb[:], scale=1.0),
                reads=[pbb, bw["bin"]], writes=[bs_sb])
            pg, pgb = next_pb()
            for jt in range(8):
                ph, phb = next_pb()
                if ph is pg:
                    ph, phb = next_pb()
                P.op("pe", lambda e, ph=ph, jt=jt: e.matmul(
                    ph[:, 0:512], w1e_sb.ap()[:, jt * 128:(jt + 1) * 128],
                    s_sb[:], start=True, stop=True),
                    reads=[bs_sb, bw["w1e"]], writes=[phb])
                hslot = jt % 2
                P.op("act", lambda e, ph=ph, jt=jt, hslot=hslot: e.activation(
                    h_sb[:, hslot, :], ph[:, 0:512], AF.Relu,
                    bias=b1f_sb[:, jt:jt + 1], scale=1.0),
                    reads=[phb, bw["b1f"]], writes=[bh])
                P.op("pe", lambda e, pg=pg, jt=jt, hslot=hslot: e.matmul(
                    pg.ap()[0:64, 0:512], w2e_sb[:, jt, :], h_sb[:, hslot, :],
                    start=(jt == 0), stop=(jt == 7)),
                    reads=[bh, bw["w2e"]], writes=[pgb])
            P.op("act", lambda e, pg=pg: e.activation(
                g_sb.ap()[0:64, :], pg.ap()[0:64, 0:512], AF.Identity,
                bias=b2_sb[:], scale=1.0),
                reads=[pgb, bw["b2"]], writes=[bg])
            for sj in range(4):
                for hf in range(2):
                    pb2, pbb2 = next_pb()

                    def fo(e, pb2=pb2, sj=sj, hf=hf):
                        e.matmul(pb2[:, 0:512],
                                 g_sb.ap()[:, sj * 128:(sj + 1) * 128],
                                 wout_sb.ap()[:, hf * 512:(hf + 1) * 512],
                                 start=True, stop=False)
                        return e.matmul(
                            pb2[:, 0:512], ones_row[:],
                            rows_bf.ap()[0:1, BO + hf * 512:BO + (hf + 1) * 512],
                            start=False, stop=True)
                    P.op("pe", fo, reads=[bg, bw["wout"], bw["ones"], bw["rows"]],
                         writes=[pbb2])
                    P.op("dve", lambda e, pb2=pb2, sj=sj, hf=hf:
                         e.scalar_tensor_tensor(
                             xc_t[:, sj, hf * 512:(hf + 1) * 512], pb2[:, 0:512],
                             0.5, xc_t[:, sj, hf * 512:(hf + 1) * 512],
                             OP.mult, OP.add),
                         reads=[pbb2], writes=[xc_buf])

        def attn_sublayer(c, xc_t, xc_buf, res_t, res_buf):
            nmu = ln_full(xc_t, xc_buf, eps5, bw["eps5"])
            for j in range(4):
                ln_apply_j(xc_t, xc_buf, nmu, j, xln_st[:], bxlnst)
                transpose_to_fm(xln_st.ap(), bxlnst, j)
            for (wsb, wbuf, bias_sb, bias_buf, dst, dbuf) in (
                    (wq_sb, bw["wq"], qb_fm, bw["qb"], qfm_sb, bqfm),
                    (wk_sb, bw["wk"], kb_fm, bw["kb"], kfm_sb, bkfm)):
                for eo in range(8):
                    pb, pbb = next_pb()

                    def fq(e, pb=pb, wsb=wsb, eo=eo):
                        r = None
                        for ei in range(8):
                            r = e.matmul(pb[:, 0:512],
                                         wsb[:, ei, eo * 128:(eo + 1) * 128],
                                         xlnfm[:, ei, :],
                                         start=(ei == 0), stop=(ei == 7))
                        return r
                    P.op("pe", fq, reads=[bxlnfm, wbuf], writes=[pbb])
                    P.op("act", lambda e, pb=pb, eo=eo, dst=dst, bias_sb=bias_sb:
                         e.activation(dst[:, eo, :], pb[:, 0:512], AF.Identity,
                                      bias=bias_sb[:, eo:eo + 1], scale=1.0),
                         reads=[pbb, bias_buf], writes=[dbuf])
            pc, pcb = next_pb()
            for eo in range(8):
                P.op("dve", lambda e, eo=eo: e.tensor_tensor(
                    attn_tm.ap()[:, 0:512], kfm_sb[:, eo, :], kfm_sb[:, eo, :],
                    OP.mult),
                    reads=[bkfm], writes=[battn_tm])

                def fc(e, eo=eo, pc=pc):
                    r = None
                    for sj in range(4):
                        r = e.matmul(pc[:, 64 + sj * 16:64 + sj * 16 + 16],
                                     attn_tm.ap()[:, sj * 128:(sj + 1) * 128],
                                     ind_sb[:, eo, :],
                                     start=(eo == 0 and sj == 0),
                                     stop=(eo == 7 and sj == 3))
                    return r
                P.op("pe", fc, reads=[battn_tm, bw["ind"]], writes=[pcb])
            P.op("act", lambda e, pc=pc: e.activation(
                cx_sb.ap()[:, :, :], pc[:, 64:128], AF.Exp, bias=0.0,
                scale=-1.0 / 128.0),
                reads=[pcb], writes=[bcx], small=True)

            for sj in range(NSUB):
                sub_g = c * NSUB + sj
                for hf in range(2):
                    pb, pbb = next_pb()

                    def fv(e, pb=pb, sj=sj, hf=hf):
                        for ei in range(8):
                            e.matmul(pb[:, 0:512],
                                     xlnfm[:, ei, sj * 128:(sj + 1) * 128],
                                     wv_sb[:, ei, hf * 512:(hf + 1) * 512],
                                     start=(ei == 0), stop=False)
                        return e.matmul(
                            pb[:, 0:512], ones_row[:],
                            rows_bf.ap()[0:1, VB + hf * 512:VB + (hf + 1) * 512],
                            start=False, stop=True)
                    P.op("pe", fv, reads=[bxlnfm, bw["wv"], bw["ones"], bw["rows"]],
                         writes=[pbb])
                    dstv = bass.AP(v_sb.ap().tensor, hf * 8 * 66,
                                   [[16 * 66, 128], [66, 8], [1, 64]])
                    P.op("act", lambda e, pb=pb, dstv=dstv: e.copy(dstv, pb[:, 0:512]),
                         reads=[pbb], writes=[bv])
                for hw in range(8):
                    pb, pbb = next_pb()

                    def fm_(e, pb=pb, hw=hw, sj=sj):
                        r = None
                        for hh in range(2):
                            for mc in range(2):
                                r = e.matmul(
                                    pb[:, hh * 256 + mc * 128:
                                       hh * 256 + (mc + 1) * 128],
                                    projz[:, hh, mc * 128:(mc + 1) * 128],
                                    qfm_sb[:, hw, sj * 128:(sj + 1) * 128],
                                    start=(hh == 0 and mc == 0),
                                    stop=(hh == 1 and mc == 1))
                        return r
                    P.op("pe", fm_, reads=[bqfm, bw["projTd"]], writes=[pbb])
                    P.op("act", lambda e, pb=pb, hw=hw: e.activation(
                        qp_sb[:, 2 * hw:2 * hw + 2, :, :], pb[:], AF.Exp,
                        bias=0.0, scale=0.125),
                        reads=[pbb], writes=[bqp])
                for hw in range(8):
                    pb, pbb = next_pb()

                    def fk(e, pb=pb, hw=hw, sj=sj):
                        r = None
                        for hh in range(2):
                            r = e.matmul(pb[:, hh * 256:(hh + 1) * 256],
                                         kfm_sb[:, hw, sj * 128:(sj + 1) * 128],
                                         projz[:, hh, :],
                                         start=(hh == 0), stop=(hh == 1))
                        return r
                    P.op("pe", fk, reads=[bkfm, bw["projTd"]], writes=[pbb])
                    P.op("act", lambda e, pb=pb, hw=hw: e.activation(
                        kpt_sb[:, 2 * hw:2 * hw + 2, :], pb[:], AF.Exp,
                        bias=0.0, scale=0.125),
                        reads=[pbb], writes=[bkpt])
                for g4 in range(4):
                    def ft(e, g4=g4):
                        for hh in range(4):
                            h = 4 * g4 + hh
                            for mc in range(2):
                                e.transpose(
                                    pbT[:, (hh * 2 + mc) * 128:
                                        (hh * 2 + mc + 1) * 128],
                                    kpt_sb.ap()[:, h, mc * 128:(mc + 1) * 128],
                                    ident[:])
                        return e.drain()
                    P.op("pe", ft, reads=[bkpt, bw["ident"]], writes=[bT])
                    P.op("dve", lambda e, g4=g4: e.tensor_copy(
                        kpf_sb[:, 4 * g4:4 * g4 + 4, :, :].bitcast(I32),
                        pbT.ap().bitcast(I32)),
                        reads=[bT], writes=[bkpf])
                P.op("dve", lambda e, sj=sj: e.tensor_tensor(
                    bass.AP(vs_sb.ap().tensor, 0,
                            [[16 * 66, 128], [66, 16], [1, 64]]),
                    bass.AP(v_sb.ap().tensor, 0,
                            [[16 * 66, 128], [66, 16], [1, 64]]),
                    bass.AP(cx_sb.ap().tensor, sj * 16,
                            [[4 * 16, 128], [1, 16], [0, 64]]),
                    OP.mult),
                    reads=[bv, bcx], writes=[bvs])
                P.op("dve", lambda e, sj=sj: e.tensor_copy(
                    bass.AP(vs_sb.ap().tensor, 64,
                            [[16 * 66, 128], [66, 16], [1, 1]]),
                    cx_sb[:, sj, :]),
                    reads=[bcx], writes=[bvs], small=True)
                def fks(e, sj=sj):
                    r = None
                    for h in range(16):
                        for mc in range(2):
                            u = 2 * h + mc
                            r = e.matmul(pbC[:, 260 + u:260 + u + 1],
                                         kpt_sb.ap()[:, h, mc * 128:(mc + 1) * 128],
                                         cx_sb[:, sj, h:h + 1],
                                         start=(u == 0), stop=(u == 31))
                    return r
                P.op("pe", fks, reads=[bkpt, bcx], writes=[bC])
                P.op("dve", lambda e: e.tensor_tensor(
                    ksum_sb[:, 0:32], ksum_sb[:, 0:32], pbC[:, 260:292], OP.add),
                    reads=[bC], writes=[bksum], small=True)
                pbS = [(pbA, bA), (pbB, bB_)]
                for w in range(4):
                    ps, psb = pbS[w % 2]
                    wslot = w % 2

                    def fsc(e, ps=ps, w=w):
                        r = None
                        for hw in range(4):
                            h = 4 * w + hw
                            for mc in range(2):
                                r = e.matmul(ps[:, hw * 128:(hw + 1) * 128],
                                             kpf_sb[:, h, mc, :],
                                             qp_sb[:, h, mc, :],
                                             start=(hw == 0 and mc == 0),
                                             stop=(hw == 3 and mc == 1))
                        return r
                    P.op("pe", fsc, reads=[bkpf, bqp], writes=[psb])
                    P.op("dve", lambda e, ps=ps, wslot=wslot: e.tensor_tensor(
                        scores_sb[:, wslot, :, :], ps[:], mask4[:], OP.mult),
                        reads=[psb, bw["mask"]], writes=[bscores[wslot]])

                    def fup(e, w=w, sub_g=sub_g):
                        r = None
                        for hw in range(4):
                            h = 4 * w + hw
                            for mc in range(2):
                                u = 2 * h + mc
                                off = 512 * (u // 8) + 64 * (u % 8)
                                st = (sub_g == 0 and (u % 8) == 0)
                                r = e.matmul(state_ps[:, off:off + 64],
                                             kpt_sb.ap()[:, h,
                                                         mc * 128:(mc + 1) * 128],
                                             vs_sb[:, h, 0:64],
                                             start=st, stop=False,
                                             skip_group_check=True)
                        return r
                    P.op("pe", fup, reads=[bkpt, bvs], writes=[bst_ps])

                    def fat(e, w=w, wslot=wslot):
                        r = None
                        for hw in range(4):
                            h = 4 * w + hw
                            r = e.matmul(pbC[:, hw * 65:hw * 65 + 65],
                                         scores_sb[:, wslot, hw, :],
                                         vs_sb[:, h, 0:65],
                                         start=(hw == 0), stop=False)
                            for mc in range(2):
                                r = e.matmul(pbC[:, hw * 65:hw * 65 + 65],
                                             qp_sb[:, h, mc, :],
                                             state_sb[:, h, mc, 0:65],
                                             start=False,
                                             stop=(hw == 3 and mc == 1))
                        return r
                    P.op("pe", fat, reads=[bscores[wslot], bvs, bqp, bstate_sb],
                         writes=[bC])
                    dens = bass.AP(pbC.ap().tensor, 64, [[512, 128], [65, 4]])
                    P.op("dve", lambda e, w=w, dens=dens: e.reciprocal(
                        recden[:, 4 * w:4 * w + 4], dens),
                        reads=[bC], writes=[brec], small=True)
                    nums = bass.AP(pbC.ap().tensor, 0,
                                   [[512, 128], [65, 4], [1, 64]])
                    P.op("dve", lambda e, w=w, nums=nums: e.tensor_tensor(
                        attn_tm[:, 4 * w * 64:(4 * w + 4) * 64], nums,
                        bass.AP(recden.ap().tensor, 4 * w,
                                [[16, 128], [1, 4], [0, 64]]),
                        OP.mult),
                        reads=[bC, brec], writes=[battn_tm])

                st_src = bass.AP(state_ps.ap().tensor, 0,
                                 [[2048, 128], [64, 32], [1, 64]])
                st_dst = bass.AP(state_sb.ap().tensor, 0,
                                 [[16 * 2 * 66, 128], [66, 32], [1, 64]])
                P.op("act", lambda e, st_src=st_src, st_dst=st_dst: e.copy(
                    st_dst, st_src),
                    reads=[bst_ps], writes=[bstate_sb])
                P.op("dve", lambda e: e.tensor_copy(
                    bass.AP(state_sb.ap().tensor, 64,
                            [[16 * 2 * 66, 128], [66, 32], [1, 1]]),
                    ksum_sb[:, 0:32]),
                    reads=[bksum], writes=[bstate_sb], small=True)

                def fatt(e):
                    for ec in range(8):
                        e.transpose(pbT[:, ec * 128:(ec + 1) * 128],
                                    attn_tm.ap()[:, ec * 128:(ec + 1) * 128],
                                    ident[:])
                    return e.drain()
                P.op("pe", fatt, reads=[battn_tm, bw["ident"]], writes=[bT])
                dstT = bass.AP(attnT.ap().tensor, sj * 128,
                               [[8 * CO, 128], [CO, 8], [1, 128]]).bitcast(I32)
                P.op("dve", lambda e, dstT=dstT: e.tensor_copy(dstT, pbT.ap().bitcast(I32)),
                     reads=[bT], writes=[battnT])

            for j in range(4):
                P.op("dve", lambda e, j=j: e.tensor_scalar(
                    res_t[:, j, :], xc_t[:, j, :], nmu[:, j:j + 1],
                    rstd_sb[:, j:j + 1], OP.add, OP.mult),
                    reads=[xc_buf, bagg, brstd], writes=[res_buf])
            for sj in range(4):
                for hf in range(2):
                    pb, pbb = next_pb()

                    def fow(e, pb=pb, sj=sj, hf=hf):
                        for ei in range(8):
                            e.matmul(pb[:, 0:512],
                                     attnT[:, ei, sj * 128:(sj + 1) * 128],
                                     wo_sb[:, ei, hf * 512:(hf + 1) * 512],
                                     start=(ei == 0), stop=False)
                        return e.matmul(
                            pb[:, 0:512], ones_row[:],
                            rows_bf.ap()[0:1, OB + hf * 512:OB + (hf + 1) * 512],
                            start=False, stop=True)
                    P.op("pe", fow, reads=[battnT, bw["wo"], bw["ones"], bw["rows"]],
                         writes=[pbb])
                    P.op("dve", lambda e, pb=pb, sj=sj, hf=hf:
                         e.scalar_tensor_tensor(
                             res_t[:, sj, hf * 512:(hf + 1) * 512], pb[:, 0:512],
                             0.0, res_t[:, sj, hf * 512:(hf + 1) * 512],
                             OP.add, OP.add),
                         reads=[pbb], writes=[res_buf])
            nmu = ln_full(res_t, res_buf, eps6, bw["eps6"])
            for j in range(4):
                P.op("dve", lambda e, j=j: e.tensor_scalar(
                    res_t[:, j, :], res_t[:, j, :], nmu[:, j:j + 1],
                    rstd_sb[:, j:j + 1], OP.add, OP.mult),
                    reads=[bagg, brstd], writes=[res_buf])
                P.op("dve", lambda e, j=j: e.tensor_tensor(
                    res_t[:, j, :], res_t[:, j, :], gam_rep[:], OP.mult),
                    reads=[bw["gam"]], writes=[res_buf])
                P.op("dve", lambda e, j=j: e.tensor_tensor(
                    xc_t[:, j, :], xc_t[:, j, :], res_t[:, j, :], OP.add),
                    reads=[res_buf], writes=[xc_buf])
                P.op("dve", lambda e, j=j: e.tensor_tensor(
                    xc_t[:, j, :], xc_t[:, j, :], bet_rep[:], OP.add),
                    reads=[bw["bet"]], writes=[xc_buf])

        P.sp_chain = False
        for c in range(n_chunks):
            cur = c % 2
            xc_t, xc_buf = xres[cur], bx[cur]
            res_t, res_buf = xres[1 - cur], bx[1 - cur]
            xsem = "x0" if c % 2 == 0 else "x1"
            xsh = P.sems[xsem].h

            def fload(e, c=c, xc_t=xc_t, xsh=xsh):
                for j in range(4):
                    src = bass.AP(d_x.tensor, (c * CO + j * 128) * E,
                                  [[E, 128], [1, E]])
                    e.dma_start(out=xc_t[:, j, :], in_=src).then_inc(xsh, 16)
                return None
            P.op("sp", fload, writes=[xc_buf], dma=True, dma_n=4, sem=xsem)
            ffn_sublayer(xc_t, xc_buf)
            attn_sublayer(c, xc_t, xc_buf, res_t, res_buf)
            ffn_sublayer(xc_t, xc_buf)
            ssem = "s0" if c % 2 == 0 else "s1"
            ssh = P.sems[ssem].h

            def fstore(e, c=c, xc_t=xc_t, ssh=ssh):
                for j in range(4):
                    dstm = bass.AP(d_y.tensor, (c * CO + j * 128) * E,
                                   [[E, 128], [1, E]])
                    e.dma_start(out=dstm, in_=xc_t[:, j, :]).then_inc(ssh, 16)
                return None
            P.op("sp", fstore, reads=[xc_buf], dma=True, dma_n=4, sem=ssem)

        if max_ops is not None:
            P.ops = P.ops[:max_ops]
        if dbg is not None:
            dbg_t = {"qfm": qfm_sb, "kfm": kfm_sb, "cx": cx_sb, "qp": qp_sb,
                     "kpt": kpt_sb, "kpf": kpf_sb, "scores": scores_sb,
                     "state": state_sb, "attn_tm": attn_tm, "attnT": attnT,
                     "v": v_sb, "vs": vs_sb, "xlnfm": xlnfm, "xlnst": xln_st,
                     "ksum": ksum_sb, "s": s_sb, "g": g_sb, "recden": recden,
                     "agg": agg_sb, "rstd": rstd_sb, "projz": projz, "stats": stats_sb,
                     "w1e": w1e_sb, "wout": wout_sb, "gam": gam_rep,
                     "xres0": xres[0], "xres1": xres[1]}[dbg]
            shp = list(dbg_t.ap().shape)
            n_free = 1
            for s_ in shp[1:]:
                n_free *= s_
            d_dbg = nc.dram_tensor("dbg_out", [128, n_free], F32,
                                   kind="ExternalOutput").ap()
            _tot = {}
            for (_e2, _f2, _w2, (semh2, amt2)) in P.ops:
                _tot[semh2] = _tot.get(semh2, 0) + amt2
            extras = [(P.sems[nm], _tot.get(P.sems[nm].h, 0)) for nm in
                      ("pe", "dve", "act", "gp")]
            flat = bass.AP(dbg_t.ap().tensor, 0, [[n_free, 128], [1, n_free]])
            P.op("gp", lambda e: e.dma_start(out=d_dbg, in_=flat),
                 extra=extras, dma=True, sem="sp")

        totals = {}
        for (_e, _f, _w, (semh, amt)) in P.ops:
            totals[semh] = totals.get(semh, 0) + amt
        finals = [(P.sems[nm].h, totals.get(P.sems[nm].h, 0))
                  for nm in ("sp", "x0", "x1", "s0", "s1")
                  if totals.get(P.sems[nm].h, 0) > 0]

        with nc.Block() as block:
            @block.sync
            def _(e):
                P.emit("sp", e, self_waits=sim_races)
                for (h, v) in finals:
                    e.wait_ge(h, v)

            @block.vector
            def _(e):
                P.emit("dve", e, self_waits=sim_races)

            @block.scalar
            def _(e):
                P.emit("act", e, self_waits=sim_races)

            @block.tensor
            def _(e):
                P.emit("pe", e, self_waits=sim_races)

            @block.gpsimd
            def _(e):
                P.emit("gp", e, self_waits=sim_races)

    return nc


def kernel(**inputs):
    x = np.ascontiguousarray(inputs["L0_attn_x"])
    nc = build_program()
    common = {
        "q_w": inputs["L0_attn_q_w"], "q_b": inputs["L0_attn_q_b"],
        "k_w": inputs["L0_attn_k_w"], "k_b": inputs["L0_attn_k_b"],
        "v_w": inputs["L0_attn_v_w"], "v_b": inputs["L0_attn_v_b"],
        "out_w": inputs["L0_attn_out_w"], "out_b": inputs["L0_attn_out_b"],
        "proj": inputs["L0_attn_proj"], "gamma": inputs["L0_attn_gamma"],
        "beta": inputs["L0_attn_beta"],
        "Win": inputs["L0_ffn_Win"], "bin_": inputs["L0_ffn_bin_"],
        "W1": inputs["L0_ffn_W1"], "b1": inputs["L0_ffn_b1"],
        "W2": inputs["L0_ffn_W2"], "b2": inputs["L0_ffn_b2"],
        "Wout": inputs["L0_ffn_Wout"], "bout": inputs["L0_ffn_bout"],
    }
    common = {k: np.ascontiguousarray(v, dtype=np.float32) for k, v in common.items()}
    in_maps = []
    for b in range(8):
        m = dict(common)
        m["x_in"] = np.ascontiguousarray(x[b])
        in_maps.append(m)
    res = run_bass_kernel_spmd(nc, in_maps, core_ids=list(range(8)))
    out = np.stack([r["x_out"] for r in res.results], axis=0).astype(np.float32)
    return out
